# version 01
# baseline (speedup 1.0000x reference)
import numpy as np
import concourse.bass as bass
import concourse.mybir as mybir
from concourse.bass_utils import run_bass_kernel_spmd

F32 = mybir.dt.float32
BF16 = mybir.dt.bfloat16
FP8 = mybir.dt.float8e4
I32 = mybir.dt.int32
ALU = mybir.AluOpType
AF = mybir.ActivationFunctionType
AX = mybir.AxisListType

S = 2048
DM = 2048
DI = 6144
H = 16
NG = 4
NIC = DM // 128
NOC = DI // 128
NTC = S // 128
TT = 512
NTT = S // TT
EPS_LN = 1e-5
EPS_AT = 1e-6

STAGE = 3


class Prog:
    ENGS = ("sync", "gpsimd", "vector", "scalar", "tensor")

    def __init__(self):
        self.items = {k: [] for k in self.ENGS}
        self.cnt = {}
        self.waited = {}

    def wait(self, eng, sem, val):
        if val is None or val <= 0:
            return
        self.waited.setdefault(sem, set()).add(val)
        self.items[eng].append(("w", sem, val))

    def op(self, eng, build, inc=None, by=1):
        val = None
        if inc is not None:
            self.cnt[inc] = self.cnt.get(inc, 0) + by
            val = self.cnt[inc]
        self.items[eng].append(("o", build, inc, by, val))
        return val

    def dma(self, eng, out, in_, inc, **kw):
        return self.op(eng, lambda e, out=out, in_=in_, kw=kw: e.dma_start(out=out, in_=in_, **kw), inc, 16)

    def val(self, sem):
        return self.cnt.get(sem, 0)

    def barrier(self):
        snap = {s: self.val(s) for s in ("t", "v", "a", "dg", "dsy")}
        for eng in self.ENGS:
            for s, v in snap.items():
                self.wait(eng, s, v)

    DMA_SEMS = ("dg", "dsy")

    def compile(self):
        remap = {}
        for s, vs in self.waited.items():
            if s in self.DMA_SEMS:
                remap[s] = {v: v for v in vs}
            else:
                remap[s] = {}
                c = 0
                for v in sorted(vs):
                    c += 1
                    remap[s][v] = c
        ops = {k: [] for k in self.ENGS}
        for eng in self.ENGS:
            for it in self.items[eng]:
                if it[0] == "w":
                    _, sem, val = it
                    tgt = remap[sem][val]
                    ops[eng].append(lambda e, SM, sem=sem, tgt=tgt: e.wait_ge(SM[sem], tgt))
                else:
                    _, build, inc, by, val = it
                    keep = inc is not None and (
                        inc in self.DMA_SEMS or val in self.waited.get(inc, ()))
                    if keep:
                        ops[eng].append(lambda e, SM, build=build, inc=inc, by=by: build(e).then_inc(SM[inc], by))
                    else:
                        ops[eng].append(lambda e, SM, build=build: build(e))
        return ops


def build_nc():
    nc = bass.Bass("TRN2", target_bir_lowering=False, detect_race_conditions=False)

    x_in = nc.dram_tensor("x_in", [S, DM], F32, kind="ExternalInput").ap()
    wq = nc.dram_tensor("wq", [DM, DM], F32, kind="ExternalInput").ap()
    wk = nc.dram_tensor("wk", [512, DM], F32, kind="ExternalInput").ap()
    wv = nc.dram_tensor("wv", [512, DM], F32, kind="ExternalInput").ap()
    wo = nc.dram_tensor("wo", [DM, DM], F32, kind="ExternalInput").ap()
    bq = nc.dram_tensor("bq", [1, DM], F32, kind="ExternalInput").ap()
    bk = nc.dram_tensor("bk", [1, 512], F32, kind="ExternalInput").ap()
    bv = nc.dram_tensor("bv", [1, 512], F32, kind="ExternalInput").ap()
    gw = nc.dram_tensor("gw", [DM, DI], F32, kind="ExternalInput").ap()
    uw = nc.dram_tensor("uw", [DM, DI], F32, kind="ExternalInput").ap()
    dw = nc.dram_tensor("dw", [DI, DM], F32, kind="ExternalInput").ap()
    gsc = nc.dram_tensor("gsc", [1, DI], F32, kind="ExternalInput").ap()
    usc = nc.dram_tensor("usc", [1, DI], F32, kind="ExternalInput").ap()
    dsc = nc.dram_tensor("dsc", [1, DM], F32, kind="ExternalInput").ap()
    y = nc.dram_tensor("y", [S, DM], F32, kind="ExternalOutput").ap()
    g8 = nc.dram_tensor("g8", [DM, DI], FP8, kind="Internal").ap()
    u8 = nc.dram_tensor("u8", [DM, DI], FP8, kind="Internal").ap()
    d8 = nc.dram_tensor("d8", [DI, DM], FP8, kind="Internal").ap()

    from contextlib import ExitStack
    es = ExitStack()
    with es:
        xr = es.enter_context(nc.sbuf_tensor("xr", [128, NTC * DM], BF16))
        A = es.enter_context(nc.sbuf_tensor("A", [128, 32768], BF16))
        B = es.enter_context(nc.sbuf_tensor("B", [128, 32768], BF16))
        C = es.enter_context(nc.sbuf_tensor("C", [128, 7168], BF16))
        SMALL = es.enter_context(nc.sbuf_tensor("SMALL", [128, 176], F32))
        ONESB = es.enter_context(nc.sbuf_tensor("ONESB", [128, 2], BF16))
        ONESF = es.enter_context(nc.sbuf_tensor("ONESF", [1, 128], F32))
        IDN = es.enter_context(nc.sbuf_tensor("IDN", [128, 128], BF16))

        PS = [es.enter_context(nc.psum_tensor("P%d" % i, [128, 512], F32)) for i in range(8)]

        sems = {}
        for s in ("dg", "dsy", "t", "v", "a", "g"):
            sems[s] = es.enter_context(nc.semaphore("sem_" + s))

        P = Prog()

        Bf8 = B.bitcast(FP8)
        Bf32 = B.bitcast(F32)
        Cf8 = C.bitcast(FP8)
        Cf32 = C.bitcast(F32)
        Ci32 = C.bitcast(I32)

        GS, US, BQc, NBQ = 0, 48, 96, 112
        SSUM, SSQ, SMEAN, SEX2, SM2, SVPE, SSQT, SRSTD, SNMR = (
            128, 132, 136, 140, 144, 148, 152, 156, 160)

        ps_guard = {"v": [0] * 8, "a": [0] * 8}

        def open_bank(b):
            P.wait("tensor", "v", ps_guard["v"][b])
            P.wait("tensor", "a", ps_guard["a"][b])

        def close_bank(b, sem, val):
            ps_guard[sem][b] = max(ps_guard[sem][b], val)

        idn_cell = [0]
        prev_ln_chain_v = [0]

        def transpose_tile(src_ap, dst_ap, wait_v=None, wait_dg=None):
            idx = transpose_tile.ctr % 2
            transpose_tile.ctr += 1
            pb = 6 + idx
            pv = PS[pb].bitcast(BF16)
            if transpose_tile.ctr == 1:
                P.wait("tensor", "v", idn_cell[0])
            if wait_v:
                P.wait("tensor", "v", wait_v)
            if wait_dg:
                P.wait("tensor", "dg", wait_dg)
            open_bank(pb)
            P.op("tensor", lambda e, pv=pv, src_ap=src_ap: e.transpose(
                pv[:, 0:128], src_ap, IDN[:]), inc="t")
            t_tr = P.val("t")
            P.wait("scalar", "t", t_tr)
            P.op("scalar", lambda e, pv=pv, dst_ap=dst_ap: e.copy(
                out=dst_ap, in_=pv[:, 0:128]), inc="a")
            close_bank(pb, "a", P.val("a"))
            return P.val("a"), t_tr
        transpose_tile.ctr = 0

        def ln_group(tc0, dst_pos):
            for i in range(4):
                tc = tc0 + i
                P.op("vector", lambda e, i=i, tc=tc: e.tensor_reduce(
                    out=SMALL[:, SSUM + i:SSUM + i + 1], in_=xr[:, tc * DM:(tc + 1) * DM],
                    axis=AX.X, op=ALU.add), inc="v")
                P.wait("scalar", "v", prev_ln_chain_v[0])
                P.op("scalar", lambda e, i=i, tc=tc: e.activation(
                    out=C[:, 4096:6144], in_=xr[:, tc * DM:(tc + 1) * DM], func=AF.Square,
                    accum_out=SMALL[:, SSQ + i:SSQ + i + 1]), inc="a")
            v_stats = P.val("v")
            a_stats = P.val("a")
            P.wait("vector", "v", v_stats)
            P.wait("vector", "a", a_stats)
            P.op("vector", lambda e: e.tensor_scalar(
                out=SMALL[:, SMEAN:SMEAN + 4], in0=SMALL[:, SSUM:SSUM + 4],
                scalar1=1.0 / DM, scalar2=None, op0=ALU.mult), inc="v")
            P.op("vector", lambda e: e.tensor_scalar(
                out=SMALL[:, SEX2:SEX2 + 4], in0=SMALL[:, SSQ:SSQ + 4],
                scalar1=1.0 / DM, scalar2=None, op0=ALU.mult), inc="v")
            P.wait("vector", "v", P.val("v"))
            P.op("vector", lambda e: e.tensor_tensor(
                out=SMALL[:, SM2:SM2 + 4], in0=SMALL[:, SMEAN:SMEAN + 4],
                in1=SMALL[:, SMEAN:SMEAN + 4], op=ALU.mult), inc="v")
            P.wait("vector", "v", P.val("v"))
            P.op("vector", lambda e: e.tensor_tensor(
                out=SMALL[:, SVPE:SVPE + 4], in0=SMALL[:, SEX2:SEX2 + 4],
                in1=SMALL[:, SM2:SM2 + 4], op=ALU.subtract), inc="v")
            P.wait("vector", "v", P.val("v"))
            P.op("vector", lambda e: e.tensor_scalar(
                out=SMALL[:, SVPE:SVPE + 4], in0=SMALL[:, SVPE:SVPE + 4],
                scalar1=EPS_LN, scalar2=None, op0=ALU.add), inc="v")
            v_chain = P.val("v")
            P.wait("scalar", "v", v_chain)
            P.op("scalar", lambda e: e.activation(
                out=SMALL[:, SSQT:SSQT + 4], in_=SMALL[:, SVPE:SVPE + 4], func=AF.Sqrt), inc="a")
            a_sqrt = P.val("a")
            P.wait("vector", "a", a_sqrt)
            P.op("vector", lambda e: e.reciprocal(
                out=SMALL[:, SRSTD:SRSTD + 4], in_=SMALL[:, SSQT:SSQT + 4]), inc="v")
            P.wait("vector", "v", P.val("v"))
            P.op("vector", lambda e: e.tensor_tensor(
                out=SMALL[:, SNMR:SNMR + 4], in0=SMALL[:, SMEAN:SMEAN + 4],
                in1=SMALL[:, SRSTD:SRSTD + 4], op=ALU.mult), inc="v")
            P.wait("vector", "v", P.val("v"))
            P.op("vector", lambda e: e.tensor_scalar(
                out=SMALL[:, SNMR:SNMR + 4], in0=SMALL[:, SNMR:SNMR + 4],
                scalar1=-1.0, scalar2=None, op0=ALU.mult), inc="v")
            v_nmr = P.val("v")
            prev_ln_chain_v[0] = v_nmr
            for i in range(4):
                tc = tc0 + i
                P.wait("vector", "v", v_nmr)
                P.wait("vector", "a", a_stats)
                P.op("vector", lambda e, i=i, tc=tc: e.tensor_scalar(
                    out=C[:, 4096:6144], in0=xr[:, tc * DM:(tc + 1) * DM],
                    scalar1=SMALL[:, SRSTD + i:SRSTD + i + 1],
                    scalar2=SMALL[:, SNMR + i:SNMR + i + 1],
                    op0=ALU.mult, op1=ALU.add), inc="v")
                v_apply = P.val("v")
                last_a = 0
                for ic in range(NIC):
                    last_a, _ = transpose_tile(
                        C[:, 4096 + ic * 128:4096 + (ic + 1) * 128],
                        A[:, dst_pos(ic, i):dst_pos(ic, i) + 128],
                        wait_v=v_apply if ic == 0 else None)
                P.wait("vector", "a", last_a)

        for tc in range(NTC):
            P.dma("gpsimd", xr[:, tc * DM:(tc + 1) * DM], x_in[tc * 128:(tc + 1) * 128, :], "dg")
        dg_x = P.val("dg")

        P.dma("sync", SMALL[:, GS:GS + 48],
              gsc[0:1, :].rearrange("o (c p) -> p (o c)", p=128), "dsy",
              allow_slow_non_contiguous=True)
        P.dma("sync", SMALL[:, US:US + 48],
              usc[0:1, :].rearrange("o (c p) -> p (o c)", p=128), "dsy",
              allow_slow_non_contiguous=True)
        P.dma("sync", SMALL[:, BQc:BQc + 16],
              bq[0:1, :].rearrange("o (c p) -> p (o c)", p=128), "dsy",
              allow_slow_non_contiguous=True)
        dsy_consts = P.val("dsy")

        P.op("vector", lambda e: e.memset(ONESF[:], 1.0))
        P.op("gpsimd", lambda e: e.iota(
            bass.AP(Ci32, 0, [[3584, 128], [1, 128]]), [[1, 128]],
            channel_multiplier=0))
        P.op("gpsimd", lambda e: e.iota(
            bass.AP(Ci32, 128, [[3584, 128], [1, 128]]), [[0, 128]],
            channel_multiplier=1), inc="g")
        g_iota = P.val("g")
        P.wait("vector", "g", g_iota)
        P.op("vector", lambda e: e.tensor_tensor(
            out=Ci32[:, 0:128], in0=Ci32[:, 0:128], in1=Ci32[:, 128:256],
            op=ALU.is_equal), inc="v")
        P.wait("vector", "v", P.val("v"))
        P.op("vector", lambda e: e.tensor_copy(out=IDN[:], in_=Ci32[:, 0:128]), inc="v")
        idn_cell[0] = P.val("v")

        P.wait("vector", "dsy", dsy_consts)
        P.op("vector", lambda e: e.tensor_scalar(
            out=SMALL[:, NBQ:NBQ + 16], in0=SMALL[:, BQc:BQc + 16], scalar1=-1.0,
            scalar2=None, op0=ALU.mult), inc="v")

        qchunks = []
        for (w_ap, scr_ap, rows, cols) in ((gw, g8, DM, DI), (uw, u8, DM, DI), (dw, d8, DI, DM)):
            for rc in range(rows // 128):
                qchunks.append((w_ap, scr_ap, rc, cols))
        store_done = []
        v_cp_prev = 0
        dg_g8 = dg_u8 = 0
        for qi, (w_ap, scr_ap, rc, cols) in enumerate(qchunks):
            if qi == 0:
                P.wait("gpsimd", "v", idn_cell[0])
            P.wait("gpsimd", "dg", store_done[-1] if store_done else 0)
            P.wait("gpsimd", "v", v_cp_prev)
            P.dma("gpsimd", Cf8[:, 0:cols], w_ap[rc * 128:(rc + 1) * 128, :], "dg")
            dg_ld = P.val("dg")
            P.wait("vector", "dg", dg_ld)
            n32 = cols // 4
            P.op("vector", lambda e, n32=n32: e.tensor_copy(
                out=Ci32[:, 1792:1792 + n32], in_=Ci32[:, 0:n32]), inc="v")
            v_cp = P.val("v")
            v_cp_prev = v_cp
            P.wait("gpsimd", "v", v_cp)
            P.dma("gpsimd", scr_ap[rc * 128:(rc + 1) * 128, :], Cf8[:, 7168:7168 + cols], "dg")
            store_done.append(P.val("dg"))
            if scr_ap is g8:
                dg_g8 = P.val("dg")
            if scr_ap is u8:
                dg_u8 = P.val("dg")
        dg_d8_chunks = store_done[32:]

        def ffn_phase(phase):
            P.barrier()
            if phase == 1:
                P.wait("vector", "dg", dg_x)
                P.wait("scalar", "dg", dg_x)
            P.dma("sync", Bf32[0:1, 11776:13824], dsc, "dsy")
            dsy_ds = P.val("dsy")
            P.wait("tensor", "dsy", dsy_ds)
            for og in range(4):
                open_bank(5)
                P.op("tensor", lambda e, og=og: e.matmul(
                    PS[5][:, 0:512], ONESF[0:1, :],
                    Bf32[0:1, 11776 + og * 512:11776 + (og + 1) * 512],
                    start=True, stop=True), inc="t")
                t_b = P.val("t")
                P.wait("vector", "t", t_b)
                P.op("vector", lambda e, og=og: e.tensor_scalar(
                    out=Cf32[:, og * 512:(og + 1) * 512], in0=PS[5][:, 0:512],
                    scalar1=0.5, scalar2=None, op0=ALU.mult), inc="v")
                close_bank(5, "v", P.val("v"))
            v_dsb = P.val("v")

            stage_dma = [0, 0]
            slab_last_t = [0, 0]
            for tt in range(NTT):
                def dst_pos(ic, tc4):
                    return ic * 512 + tc4 * 128
                ln_group(tt * 4, dst_pos)
                a_ln = P.val("a")

                slab_dma = {}

                def issue_slab(si):
                    P.wait("sync", "t", slab_last_t[si % 2])
                    for nm, scr in (("g", g8), ("u", u8)):
                        dst = Bf8[:, (0 if nm == "g" else 16384) + (si % 2) * 8192:
                                  (0 if nm == "g" else 16384) + (si % 2) * 8192 + 8192]
                        src = bass.AP(scr.tensor, si * 512,
                                      [[DI, 128], [128 * DI, 16], [1, 512]])
                        P.dma("sync", dst, src, "dsy")
                    slab_dma[si] = P.val("dsy")

                if phase == 1 and tt == 0:
                    P.wait("sync", "dg", max(dg_g8, dg_u8))
                issue_slab(0)
                issue_slab(1)

                gact_free = [0, 0]
                for oc in range(NOC):
                    si = oc // 4
                    par = oc % 2
                    base_g = (si % 2) * 8192
                    base_u = 16384 + (si % 2) * 8192
                    coff = (oc % 4) * 128
                    if oc == 0:
                        P.wait("tensor", "a", a_ln)
                    P.wait("tensor", "dsy", slab_dma[si])
                    open_bank(par)
                    open_bank(2 + par)
                    for ic in range(NIC):
                        P.op("tensor", lambda e, par=par, base_g=base_g, coff=coff, ic=ic: e.matmul(
                            PS[par][:, 0:512],
                            Bf8[:, base_g + ic * 512 + coff: base_g + ic * 512 + coff + 128],
                            A[:, ic * 512:(ic + 1) * 512],
                            start=(ic == 0), stop=(ic == NIC - 1)), inc="t")
                    t_g = P.val("t")
                    for ic in range(NIC):
                        P.op("tensor", lambda e, par=par, base_u=base_u, coff=coff, ic=ic: e.matmul(
                            PS[2 + par][:, 0:512],
                            Bf8[:, base_u + ic * 512 + coff: base_u + ic * 512 + coff + 128],
                            A[:, ic * 512:(ic + 1) * 512],
                            start=(ic == 0), stop=(ic == NIC - 1)), inc="t")
                    t_u = P.val("t")
                    slab_last_t[si % 2] = t_u
                    if oc % 4 == 3 and si + 2 <= 11:
                        issue_slab(si + 2)
                    P.wait("scalar", "t", t_g)
                    P.wait("scalar", "v", gact_free[par])
                    P.op("scalar", lambda e, par=par, oc=oc: e.activation(
                        out=Bf32[:, 9728 + par * 512: 9728 + par * 512 + 512],
                        in_=PS[par][:, 0:512], func=AF.Silu,
                        scale=SMALL[:, GS + oc:GS + oc + 1]), inc="a")
                    a_silu = P.val("a")
                    close_bank(par, "a", a_silu)
                    P.wait("vector", "t", t_u)
                    P.wait("vector", "a", a_silu)
                    P.op("vector", lambda e, par=par, oc=oc: e.scalar_tensor_tensor(
                        out=A[:, 8192 + oc * 512: 8192 + (oc + 1) * 512],
                        in0=PS[2 + par][:, 0:512],
                        scalar=SMALL[:, US + oc:US + oc + 1],
                        in1=Bf32[:, 9728 + par * 512: 9728 + par * 512 + 512],
                        op0=ALU.mult, op1=ALU.mult), inc="v")
                    close_bank(2 + par, "v", P.val("v"))
                    gact_free[par] = P.val("v")
                v_fuse = P.val("v")

                slot_last_t = [0, 0, 0]
                for pi, tcp in enumerate(((0, 1), (2, 3))):
                    dw_dma = {}

                    def issue_dw(icnk):
                        slot = icnk % 3
                        P.wait("sync", "t", slot_last_t[slot])
                        if phase == 1 and tt == 0 and pi == 0:
                            P.wait("sync", "dg", dg_d8_chunks[icnk])
                        P.dma("sync", Bf8[:, 32768 + slot * 2048: 32768 + (slot + 1) * 2048],
                              d8[icnk * 128:(icnk + 1) * 128, :], "dsy")
                        dw_dma[icnk] = P.val("dsy")

                    issue_dw(0); issue_dw(1); issue_dw(2)
                    for icnk in range(NOC):
                        slot = icnk % 3
                        if icnk == 0:
                            P.wait("tensor", "v", v_fuse)
                            for b in range(8):
                                open_bank(b)
                        P.wait("tensor", "dsy", dw_dma[icnk])
                        for ti, tci in enumerate(tcp):
                            for og in range(4):
                                b = ti * 4 + og
                                P.op("tensor", lambda e, b=b, icnk=icnk, tci=tci, og=og, slot=slot: e.matmul(
                                    PS[b][:, 0:512],
                                    A[:, 8192 + icnk * 512 + tci * 128: 8192 + icnk * 512 + tci * 128 + 128],
                                    Bf8[:, 32768 + slot * 2048 + og * 512: 32768 + slot * 2048 + (og + 1) * 512],
                                    start=(icnk == 0), stop=(icnk == NOC - 1)), inc="t")
                        t_dn = P.val("t")
                        slot_last_t[slot] = t_dn
                        if icnk + 3 < NOC:
                            issue_dw(icnk + 3)
                    for ti, tci in enumerate(tcp):
                        tc = tt * 4 + tci
                        if phase == 3:
                            P.wait("vector", "dg", stage_dma[ti])
                        for og in range(4):
                            b = ti * 4 + og
                            P.wait("vector", "t", t_dn)
                            P.wait("vector", "v", v_dsb)
                            P.op("vector", lambda e, b=b, og=og: e.tensor_tensor(
                                out=Bf32[:, 10752 + (b % 2) * 512: 10752 + (b % 2) * 512 + 512],
                                in0=PS[b][:, 0:512], in1=Cf32[:, og * 512:(og + 1) * 512],
                                op=ALU.mult), inc="v")
                            P.wait("vector", "v", P.val("v"))
                            if phase == 1:
                                P.op("vector", lambda e, b=b, tc=tc, og=og: e.tensor_tensor(
                                    out=xr[:, tc * DM + og * 512: tc * DM + (og + 1) * 512],
                                    in0=Bf32[:, 10752 + (b % 2) * 512: 10752 + (b % 2) * 512 + 512],
                                    in1=xr[:, tc * DM + og * 512: tc * DM + (og + 1) * 512],
                                    op=ALU.add), inc="v")
                            else:
                                P.op("vector", lambda e, b=b, tc=tc, og=og, ti=ti: e.tensor_tensor(
                                    out=Bf32[:, 11776 + ti * 2048 + og * 512: 11776 + ti * 2048 + (og + 1) * 512],
                                    in0=Bf32[:, 10752 + (b % 2) * 512: 10752 + (b % 2) * 512 + 512],
                                    in1=xr[:, tc * DM + og * 512: tc * DM + (og + 1) * 512],
                                    op=ALU.add), inc="v")
                            close_bank(b, "v", P.val("v"))
                        if phase == 3:
                            v_stage = P.val("v")
                            P.wait("gpsimd", "v", v_stage)
                            P.dma("gpsimd", y[tc * 128:(tc + 1) * 128, :],
                                  Bf32[:, 11776 + ti * 2048: 11776 + (ti + 1) * 2048], "dg")
                            stage_dma[ti] = P.val("dg")

        def attention_phase():
            P.barrier()
            for g4 in range(4):
                def dst_pos(ic, tc4, g4=g4):
                    return ic * 2048 + (g4 * 4 + tc4) * 128
                ln_group(g4 * 4, dst_pos)
            a_ln2 = P.val("a")
            v_ln2 = P.val("v")

            for wi, (w_ap, boff) in enumerate(((wk, 0), (wv, 8192))):
                for oc4 in range(4):
                    pp = (wi * 4 + oc4) % 2
                    if wi == 0 and oc4 == 0:
                        P.wait("gpsimd", "v", v_ln2)
                    if wi * 4 + oc4 >= 2:
                        P.wait("gpsimd", "t", natkv_t[wi * 4 + oc4 - 2])
                    P.dma("gpsimd", C[:, pp * 2048:(pp + 1) * 2048],
                          w_ap[oc4 * 128:(oc4 + 1) * 128, :], "dg")
                    dg_nat = P.val("dg")
                    lt = 0
                    for ic in range(NIC):
                        _, lt = transpose_tile(
                            C[:, pp * 2048 + ic * 128: pp * 2048 + (ic + 1) * 128],
                            B[:, boff + ic * 512 + oc4 * 128: boff + ic * 512 + oc4 * 128 + 128],
                            wait_dg=dg_nat if ic == 0 else None)
                    natkv_t.append(lt)
            a_wkvt = P.val("a")

            P.wait("sync", "a", a_wkvt)
            P.dma("sync", Cf32[0:1, 1536:2048], bk, "dsy")
            dsy_bk = P.val("dsy")
            P.wait("tensor", "dsy", dsy_bk)
            open_bank(5)
            P.op("tensor", lambda e: e.matmul(PS[5][:, 0:512], ONESF[0:1, :],
                                              Cf32[0:1, 1536:2048],
                                              start=True, stop=True), inc="t")
            t_bk = P.val("t")
            P.wait("vector", "t", t_bk)
            P.wait("vector", "a", a_wkvt)
            P.op("vector", lambda e: e.tensor_copy(out=Cf32[:, 0:512], in_=PS[5][:, 0:512]), inc="v")
            v_bkb = P.val("v")
            close_bank(5, "v", v_bkb)
            P.wait("sync", "v", v_bkb)
            P.wait("sync", "t", t_bk)
            P.dma("sync", Cf32[0:1, 1536:2048], bv, "dsy")
            dsy_bv = P.val("dsy")
            P.wait("tensor", "dsy", dsy_bv)
            open_bank(5)
            P.op("tensor", lambda e: e.matmul(PS[5][:, 0:512], ONESF[0:1, :],
                                              Cf32[0:1, 1536:2048],
                                              start=True, stop=True), inc="t")
            t_bv = P.val("t")
            P.wait("vector", "t", t_bv)
            P.op("vector", lambda e: e.tensor_copy(out=Cf32[:, 512:1024], in_=PS[5][:, 0:512]), inc="v")
            v_bvb = P.val("v")
            close_bank(5, "v", v_bvb)

            for g in range(NG):
                P.op("vector", lambda e, g=g: e.memset(
                    C[:, 4608 + g * 129 + 128: 4608 + g * 129 + 129], 1.0),
                    inc="v" if g == NG - 1 else None)
            v_ones = P.val("v")

            kv_banks = (1, 2, 3, 4)
            for g in range(NG):
                open_bank(kv_banks[g])
            for tc in range(NTC):
                if tc == 0:
                    P.wait("tensor", "a", a_wkvt)
                    P.wait("tensor", "a", a_ln2)
                open_bank(0)
                for ic in range(NIC):
                    P.op("tensor", lambda e, ic=ic, tc=tc: e.matmul(
                        PS[0][:, 0:512],
                        A[:, ic * 2048 + tc * 128: ic * 2048 + tc * 128 + 128],
                        B[:, ic * 512:(ic + 1) * 512],
                        start=(ic == 0), stop=(ic == NIC - 1)), inc="t")
                t_k = P.val("t")
                open_bank(5)
                for ic in range(NIC):
                    P.op("tensor", lambda e, ic=ic, tc=tc: e.matmul(
                        PS[5][:, 0:512],
                        A[:, ic * 2048 + tc * 128: ic * 2048 + tc * 128 + 128],
                        B[:, 8192 + ic * 512: 8192 + (ic + 1) * 512],
                        start=(ic == 0), stop=(ic == NIC - 1)), inc="t")
                t_v = P.val("t")
                P.wait("vector", "t", t_k)
                if tc == 0:
                    P.wait("vector", "v", max(v_bkb, v_bvb, v_ones))
                P.op("vector", lambda e: e.tensor_tensor(
                    out=Cf32[:, 1024:1536], in0=PS[0][:, 0:512], in1=Cf32[:, 0:512],
                    op=ALU.add), inc="v")
                v_z = P.val("v")
                close_bank(0, "v", v_z)
                P.wait("scalar", "v", v_z)
                P.op("scalar", lambda e: e.activation(
                    out=Cf32[:, 1536:2048], in_=Cf32[:, 1024:1536], func=AF.Relu, scale=-1.0), inc="a")
                a_a = P.val("a")
                P.wait("vector", "a", a_a)
                P.op("vector", lambda e: e.tensor_tensor(
                    out=Cf32[:, 1024:1536], in0=Cf32[:, 1024:1536], in1=Cf32[:, 1536:2048],
                    op=ALU.add), inc="v")
                v_t1 = P.val("v")
                P.wait("scalar", "v", v_t1)
                P.op("scalar", lambda e: e.activation(
                    out=Cf32[:, 1536:2048], in_=Cf32[:, 1536:2048], func=AF.Exp, scale=-1.0), inc="a")
                a_b = P.val("a")
                P.wait("vector", "a", a_b)
                P.op("vector", lambda e: e.tensor_tensor(
                    out=C[:, 4096:4608], in0=Cf32[:, 1024:1536], in1=Cf32[:, 1536:2048],
                    op=ALU.add), inc="v")
                P.wait("vector", "t", t_v)
                for g in range(NG):
                    P.op("vector", lambda e, g=g: e.tensor_tensor(
                        out=C[:, 4608 + g * 129: 4608 + g * 129 + 128],
                        in0=PS[5][:, g * 128:(g + 1) * 128],
                        in1=Cf32[:, 512 + g * 128: 512 + (g + 1) * 128], op=ALU.add), inc="v")
                v_vext = P.val("v")
                close_bank(5, "v", v_vext)
                P.wait("tensor", "v", v_vext)
                for g in range(NG):
                    P.op("tensor", lambda e, g=g, tc=tc: e.matmul(
                        PS[kv_banks[g]][:, 0:129],
                        C[:, 4096 + g * 128: 4096 + (g + 1) * 128],
                        C[:, 4608 + g * 129: 4608 + (g + 1) * 129],
                        start=(tc == 0), stop=(tc == NTC - 1)), inc="t")
                t_kv = P.val("t")
                P.wait("vector", "t", t_kv)
            t_kv_done = P.val("t")
            P.wait("vector", "t", t_kv_done)
            for g in range(NG):
                P.op("vector", lambda e, g=g: e.tensor_copy(
                    out=C[:, 5632 + g * 132: 5632 + g * 132 + 129],
                    in_=PS[kv_banks[g]][:, 0:129]), inc="v")
                close_bank(kv_banks[g], "v", P.val("v"))
            v_kv = P.val("v")

            P.wait("gpsimd", "t", t_kv_done)
            P.wait("gpsimd", "v", v_kv)
            natq_dma = {}
            P.dma("gpsimd", B[:, 0:2048], wq[0:128, :], "dg")
            natq_dma[0] = P.val("dg")
            atm_t = [0, 0]
            v_sq_last = 0
            for h in range(H):
                g = h // 4
                for ic in range(NIC):
                    transpose_tile(B[:, h * 2048 + ic * 128: h * 2048 + (ic + 1) * 128],
                                   C[:, ic * 128:(ic + 1) * 128],
                                   wait_dg=natq_dma[h] if ic == 0 else None)
                a_slabq = P.val("a")
                if h + 1 < H:
                    P.dma("gpsimd", B[:, (h + 1) * 2048:(h + 2) * 2048],
                          wq[(h + 1) * 128:(h + 2) * 128, :], "dg")
                    natq_dma[h + 1] = P.val("dg")
                P.wait("tensor", "a", a_slabq)
                if h == 0:
                    P.wait("tensor", "v", v_kv)
                for tti in range(4):
                    open_bank(tti)
                    for ic in range(NIC):
                        P.op("tensor", lambda e, tti=tti, ic=ic: e.matmul(
                            PS[tti][:, 0:512],
                            C[:, ic * 128:(ic + 1) * 128],
                            A[:, ic * 2048 + tti * 512: ic * 2048 + (tti + 1) * 512],
                            start=(ic == 0), stop=(ic == NIC - 1)), inc="t")
                    t_q = P.val("t")
                    P.wait("scalar", "t", t_q)
                    P.wait("scalar", "v", v_sq_last)
                    P.op("scalar", lambda e, tti=tti, h=h: e.activation(
                        out=Cf32[:, 1536:2048], in_=PS[tti][:, 0:512], func=AF.Relu,
                        scale=-1.0, bias=SMALL[:, NBQ + h:NBQ + h + 1]), inc="a")
                    a_relu = P.val("a")
                    P.wait("vector", "a", a_relu)
                    P.op("vector", lambda e, tti=tti, h=h: e.scalar_tensor_tensor(
                        out=Cf32[:, 1024:1536], in0=PS[tti][:, 0:512],
                        scalar=SMALL[:, BQc + h:BQc + h + 1], in1=Cf32[:, 1536:2048],
                        op0=ALU.add, op1=ALU.add), inc="v")
                    v_t1 = P.val("v")
                    P.wait("scalar", "v", v_t1)
                    P.op("scalar", lambda e: e.activation(
                        out=Cf32[:, 1536:2048], in_=Cf32[:, 1536:2048], func=AF.Exp,
                        scale=-1.0), inc="a")
                    a_exp = P.val("a")
                    P.wait("vector", "a", a_exp)
                    P.op("vector", lambda e, tti=tti, h=h: e.tensor_tensor(
                        out=B[:, h * 2048 + tti * 512: h * 2048 + (tti + 1) * 512],
                        in0=Cf32[:, 1024:1536], in1=Cf32[:, 1536:2048], op=ALU.add), inc="v")
                    v_sq_last = P.val("v")
                    close_bank(tti, "v", v_sq_last)
                v_sq = P.val("v")
                grp_ev = {}
                ngrp = (NTC + 2) // 3
                for gi in range(ngrp):
                    tcs = list(range(gi * 3, min(gi * 3 + 3, NTC)))
                    bank = 4 + (gi % 2)
                    if gi >= 2:
                        P.wait("tensor", "v", grp_ev[gi - 2])
                    if gi < 2:
                        open_bank(bank)
                    if gi == 0:
                        P.wait("tensor", "v", v_sq)
                    for pos, tc in enumerate(tcs):
                        P.op("tensor", lambda e, bank=bank, pos=pos, h=h, tc=tc, g=g: e.matmul(
                            PS[bank][:, pos * 132: pos * 132 + 128],
                            B[:, h * 2048 + tc * 128: h * 2048 + (tc + 1) * 128],
                            C[:, 5632 + g * 132: 5632 + g * 132 + 128],
                            start=True, stop=True), inc="t")
                        P.op("tensor", lambda e, bank=bank, pos=pos, h=h, tc=tc, g=g: e.matmul(
                            PS[bank][:, pos * 132 + 128: pos * 132 + 129],
                            B[:, h * 2048 + tc * 128: h * 2048 + (tc + 1) * 128],
                            C[:, 5632 + g * 132 + 128: 5632 + g * 132 + 129],
                            start=True, stop=True), inc="t")
                    t_grp = P.val("t")
                    for pos, tc in enumerate(tcs):
                        par = tc % 2
                        P.wait("vector", "t", t_grp)
                        P.op("vector", lambda e, bank=bank, pos=pos: e.tensor_scalar(
                            out=Cf32[:, 3104:3105],
                            in0=PS[bank][:, pos * 132 + 128: pos * 132 + 129],
                            scalar1=EPS_AT, scalar2=None, op0=ALU.add), inc="v")
                        P.wait("vector", "v", P.val("v"))
                        P.op("vector", lambda e: e.reciprocal(
                            out=Cf32[:, 3105:3106], in_=Cf32[:, 3104:3105]), inc="v")
                        P.wait("vector", "v", P.val("v"))
                        P.wait("vector", "t", atm_t[par])
                        P.op("vector", lambda e, bank=bank, pos=pos, par=par: e.tensor_scalar(
                            out=C[:, 4096 + par * 128: 4096 + par * 128 + 128],
                            in0=PS[bank][:, pos * 132: pos * 132 + 128],
                            scalar1=Cf32[:, 3105:3106], scalar2=None, op0=ALU.mult), inc="v")
                        v_atm = P.val("v")
                        _, t_tr = transpose_tile(
                            C[:, 4096 + par * 128: 4096 + par * 128 + 128],
                            B[:, h * 2048 + tc * 128: h * 2048 + (tc + 1) * 128],
                            wait_v=v_atm)
                        atm_t[par] = t_tr
                    grp_ev[gi] = P.val("v")
                ps_guard["v"][4] = max(ps_guard["v"][4], grp_ev.get(ngrp - 2, 0))
                ps_guard["v"][5] = max(ps_guard["v"][5], grp_ev.get(ngrp - 1, 0))
                if h + 1 < H:
                    pass
            a_attn = P.val("a")
            v_attn = P.val("v")

            P.wait("gpsimd", "t", P.val("t"))
            P.wait("gpsimd", "v", v_attn)
            natwo_dma = {}
            natwo_t = []
            for oc in range(NIC):
                pp = oc % 2
                if oc >= 2:
                    P.wait("gpsimd", "t", natwo_t[oc - 2])
                P.dma("gpsimd", C[:, pp * 2048:(pp + 1) * 2048], wo[oc * 128:(oc + 1) * 128, :], "dg")
                natwo_dma[oc] = P.val("dg")
                lt = 0
                for ic in range(NIC):
                    _, lt = transpose_tile(
                        C[:, pp * 2048 + ic * 128: pp * 2048 + (ic + 1) * 128],
                        A[:, ic * 2048 + oc * 128: ic * 2048 + (oc + 1) * 128],
                        wait_dg=natwo_dma[oc] if ic == 0 else None)
                natwo_t.append(lt)
            a_wot = P.val("a")

            for pi in range(8):
                tcp = (pi * 2, pi * 2 + 1)
                for h in range(H):
                    if h == 0:
                        P.wait("tensor", "a", max(a_wot, a_attn))
                        for b in range(8):
                            open_bank(b)
                    for ti, tci in enumerate(tcp):
                        for og in range(4):
                            b = ti * 4 + og
                            P.op("tensor", lambda e, b=b, h=h, tci=tci, og=og: e.matmul(
                                PS[b][:, 0:512],
                                B[:, h * 2048 + tci * 128: h * 2048 + (tci + 1) * 128],
                                A[:, h * 2048 + og * 512: h * 2048 + (og + 1) * 512],
                                start=(h == 0), stop=(h == H - 1)), inc="t")
                t_op = P.val("t")
                for ti, tci in enumerate(tcp):
                    for og in range(4):
                        b = ti * 4 + og
                        P.wait("vector", "t", t_op)
                        P.op("vector", lambda e, b=b, tci=tci, og=og: e.tensor_tensor(
                            out=xr[:, tci * DM + og * 512: tci * DM + (og + 1) * 512],
                            in0=PS[b][:, 0:512],
                            in1=xr[:, tci * DM + og * 512: tci * DM + (og + 1) * 512],
                            op=ALU.add), inc="v")
                        close_bank(b, "v", P.val("v"))

        natkv_t = []

        ffn_phase(1)
        if STAGE >= 2:
            attention_phase()
        if STAGE >= 3:
            ffn_phase(3)

        if STAGE < 3:
            P.barrier()
            for tc in range(NTC):
                P.dma("gpsimd", y[tc * 128:(tc + 1) * 128, :], xr[:, tc * DM:(tc + 1) * DM], "dg")

        P.wait("sync", "dg", P.val("dg"))
        P.wait("sync", "dsy", P.val("dsy"))

        ops = P.compile()
        with nc.Block() as block:
            @block.sync
            def _(e):
                for fn in ops["sync"]:
                    fn(e, sems)

            @block.gpsimd
            def _(e):
                for fn in ops["gpsimd"]:
                    fn(e, sems)

            @block.vector
            def _(e):
                for fn in ops["vector"]:
                    fn(e, sems)

            @block.scalar
            def _(e):
                for fn in ops["scalar"]:
                    fn(e, sems)

            @block.tensor
            def _(e):
                for fn in ops["tensor"]:
                    fn(e, sems)

    return nc


def kernel(L0_attn_hidden_states, L0_attn_Wq, L0_attn_bq, L0_attn_Wk, L0_attn_bk,
           L0_attn_Wv, L0_attn_bv, L0_attn_Wo, L0_ffn_gate_w, L0_ffn_up_w,
           L0_ffn_down_w, L0_ffn_gate_scale, L0_ffn_up_scale, L0_ffn_down_scale):
    x = np.ascontiguousarray(np.asarray(L0_attn_hidden_states, dtype=np.float32))
    com = {
        "wq": np.ascontiguousarray(L0_attn_Wq, dtype=np.float32),
        "wk": np.ascontiguousarray(L0_attn_Wk, dtype=np.float32),
        "wv": np.ascontiguousarray(L0_attn_Wv, dtype=np.float32),
        "wo": np.ascontiguousarray(L0_attn_Wo, dtype=np.float32),
        "bq": np.ascontiguousarray(L0_attn_bq, dtype=np.float32).reshape(1, -1),
        "bk": np.ascontiguousarray(L0_attn_bk, dtype=np.float32).reshape(1, -1),
        "bv": np.ascontiguousarray(L0_attn_bv, dtype=np.float32).reshape(1, -1),
        "gw": np.ascontiguousarray(L0_ffn_gate_w, dtype=np.float32),
        "uw": np.ascontiguousarray(L0_ffn_up_w, dtype=np.float32),
        "dw": np.ascontiguousarray(L0_ffn_down_w, dtype=np.float32),
        "gsc": np.ascontiguousarray(L0_ffn_gate_scale, dtype=np.float32).reshape(1, -1),
        "usc": np.ascontiguousarray(L0_ffn_up_scale, dtype=np.float32).reshape(1, -1),
        "dsc": np.ascontiguousarray(L0_ffn_down_scale, dtype=np.float32).reshape(1, -1),
    }
    nc = build_nc()
    in_maps = [dict(com, x_in=np.ascontiguousarray(x[b])) for b in range(8)]
    res = run_bass_kernel_spmd(nc, in_maps, core_ids=list(range(8)))
    return np.stack([np.asarray(r["y"], dtype=np.float32) for r in res.results], axis=0)


# version 08
# speedup vs baseline: 1.1534x; 1.1534x over previous
import numpy as np
import concourse.bass as bass
import concourse.mybir as mybir
from concourse.bass_utils import run_bass_kernel_spmd

F32 = mybir.dt.float32
BF16 = mybir.dt.bfloat16
FP8 = mybir.dt.float8e4
I32 = mybir.dt.int32
ALU = mybir.AluOpType
AF = mybir.ActivationFunctionType
AX = mybir.AxisListType

S = 2048
DM = 2048
DI = 6144
H = 16
NG = 4
NIC = DM // 128
NOC = DI // 128
NTC = S // 128
TT = 512
NTT = S // TT
EPS_LN = 1e-5
EPS_AT = 1e-6

STAGE = 3


class Prog:
    ENGS = ("sync", "gpsimd", "vector", "scalar", "tensor")

    def __init__(self):
        self.items = {k: [] for k in self.ENGS}
        self.cnt = {}
        self.waited = {}

    def wait(self, eng, sem, val):
        if val is None or val <= 0:
            return
        self.waited.setdefault(sem, set()).add(val)
        self.items[eng].append(("w", sem, val))

    def op(self, eng, build, inc=None, by=1):
        val = None
        if inc is not None:
            self.cnt[inc] = self.cnt.get(inc, 0) + by
            val = self.cnt[inc]
        self.items[eng].append(("o", build, inc, by, val))
        return val

    def dma(self, eng, out, in_, inc, **kw):
        return self.op(eng, lambda e, out=out, in_=in_, kw=kw: e.dma_start(out=out, in_=in_, **kw), inc, 16)

    def val(self, sem):
        return self.cnt.get(sem, 0)

    DMA_SEMS = ("dg", "dsy", "dsc", "dgx", "dga", "dgb", "dw0", "dw1", "dw2")

    def barrier(self):
        snap = {s: self.val(s) for s in ("t", "v", "a") + self.DMA_SEMS}
        for eng in self.ENGS:
            for s, v in snap.items():
                self.wait(eng, s, v)

    def compile(self):
        remap = {}
        for s, vs in self.waited.items():
            if s in self.DMA_SEMS:
                remap[s] = {v: v for v in vs}
            else:
                remap[s] = {}
                c = 0
                for v in sorted(vs):
                    c += 1
                    remap[s][v] = c
        ops = {k: [] for k in self.ENGS}
        for eng in self.ENGS:
            for it in self.items[eng]:
                if it[0] == "w":
                    _, sem, val = it
                    tgt = remap[sem][val]
                    ops[eng].append(lambda e, SM, sem=sem, tgt=tgt: e.wait_ge(SM[sem], tgt))
                else:
                    _, build, inc, by, val = it
                    keep = inc is not None and (
                        inc in self.DMA_SEMS or val in self.waited.get(inc, ()))
                    if keep:
                        ops[eng].append(lambda e, SM, build=build, inc=inc, by=by: build(e).then_inc(SM[inc], by))
                    else:
                        ops[eng].append(lambda e, SM, build=build: build(e))
        return ops


def build_nc():
    nc = bass.Bass("TRN2", target_bir_lowering=False, detect_race_conditions=False)

    x_in = nc.dram_tensor("x_in", [S, DM], F32, kind="ExternalInput").ap()
    wq = nc.dram_tensor("wq", [DM, DM], F32, kind="ExternalInput").ap()
    wk = nc.dram_tensor("wk", [512, DM], F32, kind="ExternalInput").ap()
    wv = nc.dram_tensor("wv", [512, DM], F32, kind="ExternalInput").ap()
    wo = nc.dram_tensor("wo", [DM, DM], F32, kind="ExternalInput").ap()
    bq = nc.dram_tensor("bq", [1, DM], F32, kind="ExternalInput").ap()
    bk = nc.dram_tensor("bk", [1, 512], F32, kind="ExternalInput").ap()
    bv = nc.dram_tensor("bv", [1, 512], F32, kind="ExternalInput").ap()
    gw = nc.dram_tensor("gw", [DM, DI], F32, kind="ExternalInput").ap()
    uw = nc.dram_tensor("uw", [DM, DI], F32, kind="ExternalInput").ap()
    dw = nc.dram_tensor("dw", [DI, DM], F32, kind="ExternalInput").ap()
    gsc = nc.dram_tensor("gsc", [1, DI], F32, kind="ExternalInput").ap()
    usc = nc.dram_tensor("usc", [1, DI], F32, kind="ExternalInput").ap()
    dsc = nc.dram_tensor("dsc", [1, DM], F32, kind="ExternalInput").ap()
    y = nc.dram_tensor("y", [S, DM], F32, kind="ExternalOutput").ap()
    g8 = nc.dram_tensor("g8", [12 * 128, 8192], FP8, kind="Internal").ap()
    u8 = nc.dram_tensor("u8", [12 * 128, 8192], FP8, kind="Internal").ap()
    d8 = nc.dram_tensor("d8", [DI, DM], FP8, kind="Internal").ap()

    from contextlib import ExitStack
    es = ExitStack()
    with es:
        xr = es.enter_context(nc.sbuf_tensor("xr", [128, NTC * DM], BF16))
        A = es.enter_context(nc.sbuf_tensor("A", [128, 32768], BF16))
        B = es.enter_context(nc.sbuf_tensor("B", [128, 32768], BF16))
        C = es.enter_context(nc.sbuf_tensor("C", [128, 7168], BF16))
        SMALL = es.enter_context(nc.sbuf_tensor("SMALL", [128, 176], F32))
        ONESB = es.enter_context(nc.sbuf_tensor("ONESB", [128, 2], BF16))
        ONESF = es.enter_context(nc.sbuf_tensor("ONESF", [1, 128], F32))
        IDN = es.enter_context(nc.sbuf_tensor("IDN", [128, 128], BF16))

        PS = [es.enter_context(nc.psum_tensor("P%d" % i, [128, 512], F32)) for i in range(8)]

        sems = {}
        for s in Prog.DMA_SEMS + ("t", "v", "a", "g"):
            sems[s] = es.enter_context(nc.semaphore("sem_" + s))

        P = Prog()

        Bf8 = B.bitcast(FP8)
        Bf32 = B.bitcast(F32)
        Cf8 = C.bitcast(FP8)
        Cf32 = C.bitcast(F32)
        Ci32 = C.bitcast(I32)

        GS, US, BQc, NBQ = 0, 48, 96, 112
        SSUM, SSQ, SMEAN, SEX2, SM2, SVPE, SSQT, SRSTD, SNMR = (
            128, 132, 136, 140, 144, 148, 152, 156, 160)

        ps_guard = {"v": [0] * 8, "a": [0] * 8}

        def open_bank(b):
            P.wait("tensor", "v", ps_guard["v"][b])
            P.wait("tensor", "a", ps_guard["a"][b])

        def close_bank(b, sem, val):
            ps_guard[sem][b] = max(ps_guard[sem][b], val)

        idn_cell = [0]
        prev_ln_chain_v = [0]

        def transpose_tile(src_ap, dst_ap, wait_v=None, wait_dma=None):
            idx = transpose_tile.ctr % 2
            transpose_tile.ctr += 1
            pb = 6 + idx
            pv = PS[pb].bitcast(BF16)
            if transpose_tile.ctr == 1:
                P.wait("tensor", "v", idn_cell[0])
            if wait_v:
                P.wait("tensor", "v", wait_v)
            if wait_dma:
                P.wait("tensor", *wait_dma)
            open_bank(pb)
            P.op("tensor", lambda e, pv=pv, src_ap=src_ap: e.transpose(
                pv[:, 0:128], src_ap, IDN[:]), inc="t")
            t_tr = P.val("t")
            P.wait("scalar", "t", t_tr)
            P.op("scalar", lambda e, pv=pv, dst_ap=dst_ap: e.copy(
                out=dst_ap, in_=pv[:, 0:128]), inc="a")
            close_bank(pb, "a", P.val("a"))
            return P.val("a"), t_tr
        transpose_tile.ctr = 0

        def ln_group(tc0, dst_pos):
            for i in range(4):
                tc = tc0 + i
                P.op("vector", lambda e, i=i, tc=tc: e.tensor_reduce(
                    out=SMALL[:, SSUM + i:SSUM + i + 1], in_=xr[:, tc * DM:(tc + 1) * DM],
                    axis=AX.X, op=ALU.add), inc="v")
                P.wait("scalar", "v", prev_ln_chain_v[0])
                P.op("scalar", lambda e, i=i, tc=tc: e.activation(
                    out=C[:, 4096:6144], in_=xr[:, tc * DM:(tc + 1) * DM], func=AF.Square,
                    accum_out=SMALL[:, SSQ + i:SSQ + i + 1]), inc="a")
            v_stats = P.val("v")
            a_stats = P.val("a")
            P.wait("vector", "v", v_stats)
            P.wait("vector", "a", a_stats)
            P.op("vector", lambda e: e.tensor_scalar(
                out=SMALL[:, SMEAN:SMEAN + 4], in0=SMALL[:, SSUM:SSUM + 4],
                scalar1=1.0 / DM, scalar2=None, op0=ALU.mult), inc="v")
            P.op("vector", lambda e: e.tensor_scalar(
                out=SMALL[:, SEX2:SEX2 + 4], in0=SMALL[:, SSQ:SSQ + 4],
                scalar1=1.0 / DM, scalar2=None, op0=ALU.mult), inc="v")
            P.wait("vector", "v", P.val("v"))
            P.op("vector", lambda e: e.tensor_tensor(
                out=SMALL[:, SM2:SM2 + 4], in0=SMALL[:, SMEAN:SMEAN + 4],
                in1=SMALL[:, SMEAN:SMEAN + 4], op=ALU.mult), inc="v")
            P.wait("vector", "v", P.val("v"))
            P.op("vector", lambda e: e.tensor_tensor(
                out=SMALL[:, SVPE:SVPE + 4], in0=SMALL[:, SEX2:SEX2 + 4],
                in1=SMALL[:, SM2:SM2 + 4], op=ALU.subtract), inc="v")
            P.wait("vector", "v", P.val("v"))
            P.op("vector", lambda e: e.tensor_scalar(
                out=SMALL[:, SVPE:SVPE + 4], in0=SMALL[:, SVPE:SVPE + 4],
                scalar1=EPS_LN, scalar2=None, op0=ALU.add), inc="v")
            v_chain = P.val("v")
            P.wait("scalar", "v", v_chain)
            P.op("scalar", lambda e: e.activation(
                out=SMALL[:, SSQT:SSQT + 4], in_=SMALL[:, SVPE:SVPE + 4], func=AF.Sqrt), inc="a")
            a_sqrt = P.val("a")
            P.wait("vector", "a", a_sqrt)
            P.op("vector", lambda e: e.reciprocal(
                out=SMALL[:, SRSTD:SRSTD + 4], in_=SMALL[:, SSQT:SSQT + 4]), inc="v")
            P.wait("vector", "v", P.val("v"))
            P.op("vector", lambda e: e.tensor_tensor(
                out=SMALL[:, SNMR:SNMR + 4], in0=SMALL[:, SMEAN:SMEAN + 4],
                in1=SMALL[:, SRSTD:SRSTD + 4], op=ALU.mult), inc="v")
            P.wait("vector", "v", P.val("v"))
            P.op("vector", lambda e: e.tensor_scalar(
                out=SMALL[:, SNMR:SNMR + 4], in0=SMALL[:, SNMR:SNMR + 4],
                scalar1=-1.0, scalar2=None, op0=ALU.mult), inc="v")
            v_nmr = P.val("v")
            prev_ln_chain_v[0] = v_nmr
            for i in range(4):
                tc = tc0 + i
                P.wait("vector", "v", v_nmr)
                P.wait("vector", "a", a_stats)
                P.op("vector", lambda e, i=i, tc=tc: e.tensor_scalar(
                    out=C[:, 4096:6144], in0=xr[:, tc * DM:(tc + 1) * DM],
                    scalar1=SMALL[:, SRSTD + i:SRSTD + i + 1],
                    scalar2=SMALL[:, SNMR + i:SNMR + i + 1],
                    op0=ALU.mult, op1=ALU.add), inc="v")
                v_apply = P.val("v")
                last_a = 0
                for ic in range(NIC):
                    last_a, _ = transpose_tile(
                        C[:, 4096 + ic * 128:4096 + (ic + 1) * 128],
                        A[:, dst_pos(ic, i):dst_pos(ic, i) + 128],
                        wait_v=v_apply if ic == 0 else None)
                P.wait("vector", "a", last_a)

        for tc in range(NTC):
            P.dma("gpsimd", xr[:, tc * DM:(tc + 1) * DM], x_in[tc * 128:(tc + 1) * 128, :], "dgx")
        dg_x = P.val("dgx")

        P.dma("sync", SMALL[:, GS:GS + 48],
              gsc[0:1, :].rearrange("o (c p) -> p (o c)", p=128), "dsy",
              allow_slow_non_contiguous=True)
        P.dma("sync", SMALL[:, US:US + 48],
              usc[0:1, :].rearrange("o (c p) -> p (o c)", p=128), "dsy",
              allow_slow_non_contiguous=True)
        P.dma("sync", SMALL[:, BQc:BQc + 16],
              bq[0:1, :].rearrange("o (c p) -> p (o c)", p=128), "dsy",
              allow_slow_non_contiguous=True)
        dsy_consts = P.val("dsy")

        P.op("vector", lambda e: e.memset(ONESF[:], 1.0))
        P.op("gpsimd", lambda e: e.iota(
            bass.AP(Ci32, 0, [[3584, 128], [1, 128]]), [[1, 128]],
            channel_multiplier=0))
        P.op("gpsimd", lambda e: e.iota(
            bass.AP(Ci32, 128, [[3584, 128], [1, 128]]), [[0, 128]],
            channel_multiplier=1), inc="g")
        g_iota = P.val("g")
        P.wait("vector", "g", g_iota)
        P.op("vector", lambda e: e.tensor_tensor(
            out=Ci32[:, 0:128], in0=Ci32[:, 0:128], in1=Ci32[:, 128:256],
            op=ALU.is_equal), inc="v")
        P.wait("vector", "v", P.val("v"))
        P.op("vector", lambda e: e.tensor_copy(out=IDN[:], in_=Ci32[:, 0:128]), inc="v")
        idn_cell[0] = P.val("v")

        P.wait("vector", "dsy", dsy_consts)
        P.op("vector", lambda e: e.tensor_scalar(
            out=SMALL[:, NBQ:NBQ + 16], in0=SMALL[:, BQc:BQc + 16], scalar1=-1.0,
            scalar2=None, op0=ALU.mult), inc="v")

        scr_store = {"g": [0] * 12, "u": [0] * 12}
        gu_buf_store = [0, 0]
        d8_store = [[0] * NOC, [0] * NOC]
        dw_buf_store = [0, 0, 0]

        def ffn_phase(phase):
            if phase == 1:
                P.wait("vector", "dgx", dg_x)
                P.wait("scalar", "dgx", dg_x)
                P.wait("scalar", "dsy", dsy_consts)
                P.wait("tensor", "v", idn_cell[0])
            else:
                P.barrier()
            P.dma("sync", Bf32[0:1, 13824:15872], dsc, "dsy")
            dsy_ds = P.val("dsy")
            P.wait("tensor", "dsy", dsy_ds)
            for og in range(4):
                open_bank(5)
                P.op("tensor", lambda e, og=og: e.matmul(
                    PS[5][:, 0:512], ONESF[0:1, :],
                    Bf32[0:1, 13824 + og * 512:13824 + (og + 1) * 512],
                    start=True, stop=True), inc="t")
                t_b = P.val("t")
                P.wait("vector", "t", t_b)
                P.op("vector", lambda e, og=og: e.tensor_scalar(
                    out=Cf32[:, og * 512:(og + 1) * 512], in0=PS[5][:, 0:512],
                    scalar1=0.5, scalar2=None, op0=ALU.mult), inc="v")
                close_bank(5, "v", P.val("v"))
            v_dsb = P.val("v")

            slab_last_t = [0, 0]
            slot_last_t = [0, 0, 0]
            ydma_guard = [0, 0, 0, 0]

            def dst_pos(ic, tc4):
                return ic * 512 + tc4 * 128

            LN_HOIST = True
            if LN_HOIST:
                ln_group(0, dst_pos)
            a_ln_next = P.val("a")
            for tt in range(NTT):
                first = (phase == 1 and tt == 0)
                if not LN_HOIST:
                    ln_group(tt * 4, dst_pos)
                    a_ln_next = P.val("a")
                a_ln = a_ln_next

                slab_dma = {}

                def issue_slab(si, first=first):
                    if first:
                        psem = ("dga", "dgb")[si % 2]
                        P.wait("gpsimd", "t", slab_last_t[si % 2])
                        P.wait("gpsimd", "dsy", gu_buf_store[si % 2])
                        for nm, w_src in (("g", gw), ("u", uw)):
                            boff = (0 if nm == "g" else 16384) + (si % 2) * 8192
                            src = bass.AP(w_src.tensor, si * 512,
                                          [[DI, 128], [128 * DI, 16], [1, 512]])
                            P.dma("gpsimd", Bf8[:, boff:boff + 8192], src, psem)
                        slab_dma[si] = (psem, P.val(psem))
                        P.wait("sync", psem, P.val(psem))
                        for nm, scr in (("g", g8), ("u", u8)):
                            boff = (0 if nm == "g" else 16384) + (si % 2) * 8192
                            P.dma("sync", scr[si * 128:(si + 1) * 128, :],
                                  Bf8[:, boff:boff + 8192], "dsy")
                            scr_store[nm][si] = P.val("dsy")
                        gu_buf_store[si % 2] = P.val("dsy")
                    else:
                        P.wait("sync", "t", slab_last_t[si % 2])
                        P.wait("sync", "dsy", max(scr_store["g"][si], scr_store["u"][si],
                                                  gu_buf_store[si % 2]))
                        for nm, scr in (("g", g8), ("u", u8)):
                            boff = (0 if nm == "g" else 16384) + (si % 2) * 8192
                            P.dma("sync", Bf8[:, boff:boff + 8192],
                                  scr[si * 128:(si + 1) * 128, :], "dsy")
                        slab_dma[si] = ("dsy", P.val("dsy"))

                issue_slab(0)
                issue_slab(1)

                gact_free = [0, 0]
                for oc in range(NOC):
                    si = oc // 4
                    par = oc % 2
                    base_g = (si % 2) * 8192
                    base_u = 16384 + (si % 2) * 8192
                    coff = (oc % 4) * 128
                    if oc == 0:
                        P.wait("tensor", "a", a_ln)
                    P.wait("tensor", *slab_dma[si])
                    open_bank(par)
                    open_bank(2 + par)
                    for ic in range(NIC):
                        P.op("tensor", lambda e, par=par, base_g=base_g, coff=coff, ic=ic: e.matmul(
                            PS[par][:, 0:512],
                            Bf8[:, base_g + ic * 512 + coff: base_g + ic * 512 + coff + 128],
                            A[:, ic * 512:(ic + 1) * 512],
                            start=(ic == 0), stop=(ic == NIC - 1)), inc="t")
                    t_g = P.val("t")
                    for ic in range(NIC):
                        P.op("tensor", lambda e, par=par, base_u=base_u, coff=coff, ic=ic: e.matmul(
                            PS[2 + par][:, 0:512],
                            Bf8[:, base_u + ic * 512 + coff: base_u + ic * 512 + coff + 128],
                            A[:, ic * 512:(ic + 1) * 512],
                            start=(ic == 0), stop=(ic == NIC - 1)), inc="t")
                    t_u = P.val("t")
                    slab_last_t[si % 2] = t_u
                    if oc % 4 == 3 and si + 2 <= 11:
                        issue_slab(si + 2)
                    P.wait("scalar", "t", t_g)
                    P.wait("scalar", "v", gact_free[par])
                    P.op("scalar", lambda e, par=par, oc=oc: e.activation(
                        out=Bf32[:, 9728 + par * 512: 9728 + par * 512 + 512],
                        in_=PS[par][:, 0:512], func=AF.Silu,
                        scale=SMALL[:, GS + oc:GS + oc + 1]), inc="a")
                    a_silu = P.val("a")
                    close_bank(par, "a", a_silu)
                    P.wait("vector", "t", t_u)
                    P.wait("vector", "a", a_silu)
                    P.op("vector", lambda e, par=par, oc=oc: e.scalar_tensor_tensor(
                        out=A[:, 8192 + oc * 512: 8192 + (oc + 1) * 512],
                        in0=PS[2 + par][:, 0:512],
                        scalar=SMALL[:, US + oc:US + oc + 1],
                        in1=Bf32[:, 9728 + par * 512: 9728 + par * 512 + 512],
                        op0=ALU.mult, op1=ALU.mult), inc="v")
                    close_bank(2 + par, "v", P.val("v"))
                    gact_free[par] = P.val("v")
                v_fuse = P.val("v")

                if LN_HOIST and tt + 1 < NTT:
                    ln_group((tt + 1) * 4, dst_pos)
                    a_ln_next = P.val("a")

                for pi in range(2):
                    og0 = 2 * pi
                    dw_dma = {}

                    def issue_dw(icnk, og0=og0, pi=pi, first=first):
                        slot = icnk % 3
                        if first:
                            ssem = ("dw0", "dw1", "dw2")[slot]
                            P.wait("gpsimd", "t", slot_last_t[slot])
                            P.wait("gpsimd", "dsc", dw_buf_store[slot])
                            P.dma("gpsimd", Bf8[:, 32768 + slot * 1024: 32768 + (slot + 1) * 1024],
                                  dw[icnk * 128:(icnk + 1) * 128, og0 * 512:(og0 + 2) * 512], ssem)
                            dw_dma[icnk] = (ssem, P.val(ssem))
                            P.wait("scalar", ssem, P.val(ssem))
                            P.dma("scalar", d8[icnk * 128:(icnk + 1) * 128, og0 * 512:(og0 + 2) * 512],
                                  Bf8[:, 32768 + slot * 1024: 32768 + (slot + 1) * 1024], "dsc")
                            d8_store[pi][icnk] = P.val("dsc")
                            dw_buf_store[slot] = P.val("dsc")
                        else:
                            P.wait("scalar", "t", slot_last_t[slot])
                            P.wait("scalar", "dsc", max(d8_store[pi][icnk], dw_buf_store[slot]))
                            P.dma("scalar", Bf8[:, 32768 + slot * 1024: 32768 + (slot + 1) * 1024],
                                  d8[icnk * 128:(icnk + 1) * 128, og0 * 512:(og0 + 2) * 512], "dsc")
                            dw_dma[icnk] = ("dsc", P.val("dsc"))

                    issue_dw(0); issue_dw(1); issue_dw(2)
                    for icnk in range(NOC):
                        slot = icnk % 3
                        if icnk == 0:
                            P.wait("tensor", "v", v_fuse)
                            for b in range(8):
                                open_bank(b)
                        P.wait("tensor", *dw_dma[icnk])
                        for tci in range(4):
                            for oi in range(2):
                                b = tci * 2 + oi
                                P.op("tensor", lambda e, b=b, icnk=icnk, tci=tci, oi=oi, slot=slot: e.matmul(
                                    PS[b][:, 0:512],
                                    A[:, 8192 + icnk * 512 + tci * 128: 8192 + icnk * 512 + tci * 128 + 128],
                                    Bf8[:, 32768 + slot * 1024 + oi * 512: 32768 + slot * 1024 + (oi + 1) * 512],
                                    start=(icnk == 0), stop=(icnk == NOC - 1)), inc="t")
                        t_dn = P.val("t")
                        slot_last_t[slot] = t_dn
                        if icnk + 3 < NOC:
                            issue_dw(icnk + 3)
                    for tci in range(4):
                        tc = tt * 4 + tci
                        for oi in range(2):
                            og = og0 + oi
                            b = tci * 2 + oi
                            P.wait("vector", "t", t_dn)
                            P.wait("vector", "v", v_dsb)
                            if phase == 1:
                                P.op("vector", lambda e, b=b, og=og: e.tensor_tensor(
                                    out=Bf32[:, 10752 + (b % 2) * 512: 10752 + (b % 2) * 512 + 512],
                                    in0=PS[b][:, 0:512], in1=Cf32[:, og * 512:(og + 1) * 512],
                                    op=ALU.mult), inc="v")
                                P.wait("vector", "v", P.val("v"))
                                P.op("vector", lambda e, b=b, tc=tc, og=og: e.tensor_tensor(
                                    out=xr[:, tc * DM + og * 512: tc * DM + (og + 1) * 512],
                                    in0=Bf32[:, 10752 + (b % 2) * 512: 10752 + (b % 2) * 512 + 512],
                                    in1=xr[:, tc * DM + og * 512: tc * DM + (og + 1) * 512],
                                    op=ALU.add), inc="v")
                                close_bank(b, "v", P.val("v"))
                            else:
                                slot = b % 4
                                ysem = ("dgx", "dw0", "dw1", "dw2")[slot]
                                P.wait("vector", ysem, ydma_guard[slot])
                                P.op("vector", lambda e, b=b, og=og, slot=slot: e.tensor_tensor(
                                    out=Bf32[:, 11776 + slot * 512: 11776 + (slot + 1) * 512],
                                    in0=PS[b][:, 0:512], in1=Cf32[:, og * 512:(og + 1) * 512],
                                    op=ALU.mult), inc="v")
                                close_bank(b, "v", P.val("v"))
                                P.wait("vector", "v", P.val("v"))
                                P.op("vector", lambda e, tc=tc, og=og, slot=slot: e.tensor_tensor(
                                    out=Bf32[:, 11776 + slot * 512: 11776 + (slot + 1) * 512],
                                    in0=Bf32[:, 11776 + slot * 512: 11776 + (slot + 1) * 512],
                                    in1=xr[:, tc * DM + og * 512: tc * DM + (og + 1) * 512],
                                    op=ALU.add), inc="v")
                                P.wait("gpsimd", "v", P.val("v"))
                                P.dma("gpsimd", y[tc * 128:(tc + 1) * 128, og * 512:(og + 1) * 512],
                                      Bf32[:, 11776 + slot * 512: 11776 + (slot + 1) * 512], ysem)
                                ydma_guard[slot] = P.val(ysem)

        def attention_phase():
            P.barrier()
            for g4 in range(4):
                def dst_pos(ic, tc4, g4=g4):
                    return ic * 2048 + (g4 * 4 + tc4) * 128
                ln_group(g4 * 4, dst_pos)
            a_ln2 = P.val("a")
            v_ln2 = P.val("v")

            for wi, (w_ap, boff) in enumerate(((wk, 0), (wv, 8192))):
                for oc4 in range(4):
                    pp = (wi * 4 + oc4) % 2
                    if wi == 0 and oc4 == 0:
                        P.wait("gpsimd", "v", v_ln2)
                    if wi * 4 + oc4 >= 2:
                        P.wait("gpsimd", "t", natkv_t[wi * 4 + oc4 - 2])
                    psem = ("dga", "dgb")[pp]
                    P.dma("gpsimd", C[:, pp * 2048:(pp + 1) * 2048],
                          w_ap[oc4 * 128:(oc4 + 1) * 128, :], psem)
                    dg_nat = (psem, P.val(psem))
                    lt = 0
                    for ic in range(NIC):
                        _, lt = transpose_tile(
                            C[:, pp * 2048 + ic * 128: pp * 2048 + (ic + 1) * 128],
                            B[:, boff + ic * 512 + oc4 * 128: boff + ic * 512 + oc4 * 128 + 128],
                            wait_dma=dg_nat if ic == 0 else None)
                    natkv_t.append(lt)
            a_wkvt = P.val("a")

            P.wait("sync", "a", a_wkvt)
            P.dma("sync", Cf32[0:1, 1536:2048], bk, "dsy")
            dsy_bk = P.val("dsy")
            P.wait("tensor", "dsy", dsy_bk)
            open_bank(5)
            P.op("tensor", lambda e: e.matmul(PS[5][:, 0:512], ONESF[0:1, :],
                                              Cf32[0:1, 1536:2048],
                                              start=True, stop=True), inc="t")
            t_bk = P.val("t")
            P.wait("vector", "t", t_bk)
            P.wait("vector", "a", a_wkvt)
            P.op("vector", lambda e: e.tensor_copy(out=Cf32[:, 0:512], in_=PS[5][:, 0:512]), inc="v")
            v_bkb = P.val("v")
            close_bank(5, "v", v_bkb)
            P.wait("sync", "v", v_bkb)
            P.wait("sync", "t", t_bk)
            P.dma("sync", Cf32[0:1, 1536:2048], bv, "dsy")
            dsy_bv = P.val("dsy")
            P.wait("tensor", "dsy", dsy_bv)
            open_bank(5)
            P.op("tensor", lambda e: e.matmul(PS[5][:, 0:512], ONESF[0:1, :],
                                              Cf32[0:1, 1536:2048],
                                              start=True, stop=True), inc="t")
            t_bv = P.val("t")
            P.wait("vector", "t", t_bv)
            P.op("vector", lambda e: e.tensor_copy(out=Cf32[:, 512:1024], in_=PS[5][:, 0:512]), inc="v")
            v_bvb = P.val("v")
            close_bank(5, "v", v_bvb)

            for g in range(NG):
                P.op("vector", lambda e, g=g: e.memset(
                    C[:, 4608 + g * 129 + 128: 4608 + g * 129 + 129], 1.0),
                    inc="v" if g == NG - 1 else None)
            v_ones = P.val("v")

            kv_banks = (1, 2, 3, 4)
            for g in range(NG):
                open_bank(kv_banks[g])
            for tc in range(NTC):
                if tc == 0:
                    P.wait("tensor", "a", a_wkvt)
                    P.wait("tensor", "a", a_ln2)
                open_bank(0)
                for ic in range(NIC):
                    P.op("tensor", lambda e, ic=ic, tc=tc: e.matmul(
                        PS[0][:, 0:512],
                        A[:, ic * 2048 + tc * 128: ic * 2048 + tc * 128 + 128],
                        B[:, ic * 512:(ic + 1) * 512],
                        start=(ic == 0), stop=(ic == NIC - 1)), inc="t")
                t_k = P.val("t")
                open_bank(5)
                for ic in range(NIC):
                    P.op("tensor", lambda e, ic=ic, tc=tc: e.matmul(
                        PS[5][:, 0:512],
                        A[:, ic * 2048 + tc * 128: ic * 2048 + tc * 128 + 128],
                        B[:, 8192 + ic * 512: 8192 + (ic + 1) * 512],
                        start=(ic == 0), stop=(ic == NIC - 1)), inc="t")
                t_v = P.val("t")
                P.wait("vector", "t", t_k)
                if tc == 0:
                    P.wait("vector", "v", max(v_bkb, v_bvb, v_ones))
                P.op("vector", lambda e: e.tensor_tensor(
                    out=Cf32[:, 1024:1536], in0=PS[0][:, 0:512], in1=Cf32[:, 0:512],
                    op=ALU.add), inc="v")
                v_z = P.val("v")
                close_bank(0, "v", v_z)
                P.wait("scalar", "v", v_z)
                P.op("scalar", lambda e: e.activation(
                    out=Cf32[:, 1536:2048], in_=Cf32[:, 1024:1536], func=AF.Relu, scale=-1.0), inc="a")
                a_a = P.val("a")
                P.wait("vector", "a", a_a)
                P.op("vector", lambda e: e.tensor_tensor(
                    out=Cf32[:, 1024:1536], in0=Cf32[:, 1024:1536], in1=Cf32[:, 1536:2048],
                    op=ALU.add), inc="v")
                v_t1 = P.val("v")
                P.wait("scalar", "v", v_t1)
                P.op("scalar", lambda e: e.activation(
                    out=Cf32[:, 1536:2048], in_=Cf32[:, 1536:2048], func=AF.Exp, scale=-1.0), inc="a")
                a_b = P.val("a")
                P.wait("vector", "a", a_b)
                P.op("vector", lambda e: e.tensor_tensor(
                    out=C[:, 4096:4608], in0=Cf32[:, 1024:1536], in1=Cf32[:, 1536:2048],
                    op=ALU.add), inc="v")
                P.wait("vector", "t", t_v)
                for g in range(NG):
                    P.op("vector", lambda e, g=g: e.tensor_tensor(
                        out=C[:, 4608 + g * 129: 4608 + g * 129 + 128],
                        in0=PS[5][:, g * 128:(g + 1) * 128],
                        in1=Cf32[:, 512 + g * 128: 512 + (g + 1) * 128], op=ALU.add), inc="v")
                v_vext = P.val("v")
                close_bank(5, "v", v_vext)
                P.wait("tensor", "v", v_vext)
                for g in range(NG):
                    P.op("tensor", lambda e, g=g, tc=tc: e.matmul(
                        PS[kv_banks[g]][:, 0:129],
                        C[:, 4096 + g * 128: 4096 + (g + 1) * 128],
                        C[:, 4608 + g * 129: 4608 + (g + 1) * 129],
                        start=(tc == 0), stop=(tc == NTC - 1)), inc="t")
                t_kv = P.val("t")
                P.wait("vector", "t", t_kv)
            t_kv_done = P.val("t")
            P.wait("vector", "t", t_kv_done)
            for g in range(NG):
                P.op("vector", lambda e, g=g: e.tensor_copy(
                    out=C[:, 5632 + g * 132: 5632 + g * 132 + 129],
                    in_=PS[kv_banks[g]][:, 0:129]), inc="v")
                close_bank(kv_banks[g], "v", P.val("v"))
            v_kv = P.val("v")

            P.wait("gpsimd", "t", t_kv_done)
            P.wait("gpsimd", "v", v_kv)
            natq_dma = {}
            P.dma("gpsimd", B[:, 0:2048], wq[0:128, :], "dga")
            natq_dma[0] = ("dga", P.val("dga"))
            t_slabq = [0] * H
            atm_t = [0, 0]
            v_sq_last = 0
            for h in range(H):
                g = h // 4
                lt = 0
                for ic in range(NIC):
                    _, lt = transpose_tile(B[:, h * 2048 + ic * 128: h * 2048 + (ic + 1) * 128],
                                           C[:, ic * 128:(ic + 1) * 128],
                                           wait_dma=natq_dma[h] if ic == 0 else None)
                t_slabq[h] = lt
                a_slabq = P.val("a")
                if h + 1 < H:
                    psem = ("dga", "dgb")[(h + 1) % 2]
                    if h >= 1:
                        P.wait("gpsimd", "t", t_slabq[h - 1])
                    P.dma("gpsimd", B[:, (h + 1) * 2048:(h + 2) * 2048],
                          wq[(h + 1) * 128:(h + 2) * 128, :], psem)
                    natq_dma[h + 1] = (psem, P.val(psem))
                P.wait("tensor", "a", a_slabq)
                if h == 0:
                    P.wait("tensor", "v", v_kv)
                for tti in range(4):
                    open_bank(tti)
                    for ic in range(NIC):
                        P.op("tensor", lambda e, tti=tti, ic=ic: e.matmul(
                            PS[tti][:, 0:512],
                            C[:, ic * 128:(ic + 1) * 128],
                            A[:, ic * 2048 + tti * 512: ic * 2048 + (tti + 1) * 512],
                            start=(ic == 0), stop=(ic == NIC - 1)), inc="t")
                    t_q = P.val("t")
                    P.wait("scalar", "t", t_q)
                    P.wait("scalar", "v", v_sq_last)
                    P.op("scalar", lambda e, tti=tti, h=h: e.activation(
                        out=Cf32[:, 1536:2048], in_=PS[tti][:, 0:512], func=AF.Relu,
                        scale=-1.0, bias=SMALL[:, NBQ + h:NBQ + h + 1]), inc="a")
                    a_relu = P.val("a")
                    P.wait("vector", "a", a_relu)
                    P.op("vector", lambda e, tti=tti, h=h: e.scalar_tensor_tensor(
                        out=Cf32[:, 1024:1536], in0=PS[tti][:, 0:512],
                        scalar=SMALL[:, BQc + h:BQc + h + 1], in1=Cf32[:, 1536:2048],
                        op0=ALU.add, op1=ALU.add), inc="v")
                    v_t1 = P.val("v")
                    P.wait("scalar", "v", v_t1)
                    P.op("scalar", lambda e: e.activation(
                        out=Cf32[:, 1536:2048], in_=Cf32[:, 1536:2048], func=AF.Exp,
                        scale=-1.0), inc="a")
                    a_exp = P.val("a")
                    P.wait("vector", "a", a_exp)
                    P.op("vector", lambda e, tti=tti, h=h: e.tensor_tensor(
                        out=B[:, h * 2048 + tti * 512: h * 2048 + (tti + 1) * 512],
                        in0=Cf32[:, 1024:1536], in1=Cf32[:, 1536:2048], op=ALU.add), inc="v")
                    v_sq_last = P.val("v")
                    close_bank(tti, "v", v_sq_last)
                v_sq = P.val("v")
                grp_ev = {}
                ngrp = (NTC + 2) // 3
                for gi in range(ngrp):
                    tcs = list(range(gi * 3, min(gi * 3 + 3, NTC)))
                    bank = 4 + (gi % 2)
                    if gi >= 2:
                        P.wait("tensor", "v", grp_ev[gi - 2])
                    if gi < 2:
                        open_bank(bank)
                    if gi == 0:
                        P.wait("tensor", "v", v_sq)
                    for pos, tc in enumerate(tcs):
                        P.op("tensor", lambda e, bank=bank, pos=pos, h=h, tc=tc, g=g: e.matmul(
                            PS[bank][:, pos * 132: pos * 132 + 128],
                            B[:, h * 2048 + tc * 128: h * 2048 + (tc + 1) * 128],
                            C[:, 5632 + g * 132: 5632 + g * 132 + 128],
                            start=True, stop=True), inc="t")
                        P.op("tensor", lambda e, bank=bank, pos=pos, h=h, tc=tc, g=g: e.matmul(
                            PS[bank][:, pos * 132 + 128: pos * 132 + 129],
                            B[:, h * 2048 + tc * 128: h * 2048 + (tc + 1) * 128],
                            C[:, 5632 + g * 132 + 128: 5632 + g * 132 + 129],
                            start=True, stop=True), inc="t")
                    t_grp = P.val("t")
                    for pos, tc in enumerate(tcs):
                        par = tc % 2
                        P.wait("vector", "t", t_grp)
                        P.op("vector", lambda e, bank=bank, pos=pos: e.tensor_scalar(
                            out=Cf32[:, 3104:3105],
                            in0=PS[bank][:, pos * 132 + 128: pos * 132 + 129],
                            scalar1=EPS_AT, scalar2=None, op0=ALU.add), inc="v")
                        P.wait("vector", "v", P.val("v"))
                        P.op("vector", lambda e: e.reciprocal(
                            out=Cf32[:, 3105:3106], in_=Cf32[:, 3104:3105]), inc="v")
                        P.wait("vector", "v", P.val("v"))
                        P.wait("vector", "t", atm_t[par])
                        P.op("vector", lambda e, bank=bank, pos=pos, par=par: e.tensor_scalar(
                            out=C[:, 4096 + par * 128: 4096 + par * 128 + 128],
                            in0=PS[bank][:, pos * 132: pos * 132 + 128],
                            scalar1=Cf32[:, 3105:3106], scalar2=None, op0=ALU.mult), inc="v")
                        v_atm = P.val("v")
                        _, t_tr = transpose_tile(
                            C[:, 4096 + par * 128: 4096 + par * 128 + 128],
                            B[:, h * 2048 + tc * 128: h * 2048 + (tc + 1) * 128],
                            wait_v=v_atm)
                        atm_t[par] = t_tr
                    grp_ev[gi] = P.val("v")
                ps_guard["v"][4] = max(ps_guard["v"][4], grp_ev.get(ngrp - 2, 0))
                ps_guard["v"][5] = max(ps_guard["v"][5], grp_ev.get(ngrp - 1, 0))
                if h + 1 < H:
                    pass
            a_attn = P.val("a")
            v_attn = P.val("v")

            P.wait("gpsimd", "t", P.val("t"))
            P.wait("gpsimd", "v", v_attn)
            natwo_dma = {}
            natwo_t = []
            for oc in range(NIC):
                pp = oc % 2
                if oc >= 2:
                    P.wait("gpsimd", "t", natwo_t[oc - 2])
                psem = ("dga", "dgb")[pp]
                P.dma("gpsimd", C[:, pp * 2048:(pp + 1) * 2048], wo[oc * 128:(oc + 1) * 128, :], psem)
                natwo_dma[oc] = (psem, P.val(psem))
                lt = 0
                for ic in range(NIC):
                    _, lt = transpose_tile(
                        C[:, pp * 2048 + ic * 128: pp * 2048 + (ic + 1) * 128],
                        A[:, ic * 2048 + oc * 128: ic * 2048 + (oc + 1) * 128],
                        wait_dma=natwo_dma[oc] if ic == 0 else None)
                natwo_t.append(lt)
            a_wot = P.val("a")

            OP16 = True
            npass = 16 if OP16 else 8
            for pi in range(npass):
                tcl = (pi,) if OP16 else (pi * 2, pi * 2 + 1)
                base = (pi % 2) * 4 if OP16 else 0
                for h in range(H):
                    if h == 0:
                        if pi == 0:
                            P.wait("tensor", "a", max(a_wot, a_attn))
                        nb = 4 if OP16 else 8
                        for b in range(base, base + nb):
                            open_bank(b)
                    for ti, tci in enumerate(tcl):
                        for og in range(4):
                            b = base + ti * 4 + og
                            P.op("tensor", lambda e, b=b, h=h, tci=tci, og=og: e.matmul(
                                PS[b][:, 0:512],
                                B[:, h * 2048 + tci * 128: h * 2048 + (tci + 1) * 128],
                                A[:, h * 2048 + og * 512: h * 2048 + (og + 1) * 512],
                                start=(h == 0), stop=(h == H - 1)), inc="t")
                t_op = P.val("t")
                for ti, tci in enumerate(tcl):
                    for og in range(4):
                        b = base + ti * 4 + og
                        P.wait("vector", "t", t_op)
                        P.op("vector", lambda e, b=b, tci=tci, og=og: e.tensor_tensor(
                            out=xr[:, tci * DM + og * 512: tci * DM + (og + 1) * 512],
                            in0=PS[b][:, 0:512],
                            in1=xr[:, tci * DM + og * 512: tci * DM + (og + 1) * 512],
                            op=ALU.add), inc="v")
                        close_bank(b, "v", P.val("v"))

        natkv_t = []

        ffn_phase(1)
        if STAGE >= 2:
            attention_phase()
        if STAGE >= 3:
            ffn_phase(3)

        if STAGE < 3:
            P.barrier()
            for tc in range(NTC):
                P.dma("gpsimd", y[tc * 128:(tc + 1) * 128, :], xr[:, tc * DM:(tc + 1) * DM], "dg")

        for s in Prog.DMA_SEMS:
            P.wait("sync", s, P.val(s))

        ops = P.compile()
        with nc.Block() as block:
            @block.sync
            def _(e):
                for fn in ops["sync"]:
                    fn(e, sems)

            @block.gpsimd
            def _(e):
                for fn in ops["gpsimd"]:
                    fn(e, sems)

            @block.vector
            def _(e):
                for fn in ops["vector"]:
                    fn(e, sems)

            @block.scalar
            def _(e):
                for fn in ops["scalar"]:
                    fn(e, sems)

            @block.tensor
            def _(e):
                for fn in ops["tensor"]:
                    fn(e, sems)

    return nc


def kernel(L0_attn_hidden_states, L0_attn_Wq, L0_attn_bq, L0_attn_Wk, L0_attn_bk,
           L0_attn_Wv, L0_attn_bv, L0_attn_Wo, L0_ffn_gate_w, L0_ffn_up_w,
           L0_ffn_down_w, L0_ffn_gate_scale, L0_ffn_up_scale, L0_ffn_down_scale):
    x = np.ascontiguousarray(np.asarray(L0_attn_hidden_states, dtype=np.float32))
    com = {
        "wq": np.ascontiguousarray(L0_attn_Wq, dtype=np.float32),
        "wk": np.ascontiguousarray(L0_attn_Wk, dtype=np.float32),
        "wv": np.ascontiguousarray(L0_attn_Wv, dtype=np.float32),
        "wo": np.ascontiguousarray(L0_attn_Wo, dtype=np.float32),
        "bq": np.ascontiguousarray(L0_attn_bq, dtype=np.float32).reshape(1, -1),
        "bk": np.ascontiguousarray(L0_attn_bk, dtype=np.float32).reshape(1, -1),
        "bv": np.ascontiguousarray(L0_attn_bv, dtype=np.float32).reshape(1, -1),
        "gw": np.ascontiguousarray(L0_ffn_gate_w, dtype=np.float32),
        "uw": np.ascontiguousarray(L0_ffn_up_w, dtype=np.float32),
        "dw": np.ascontiguousarray(L0_ffn_down_w, dtype=np.float32),
        "gsc": np.ascontiguousarray(L0_ffn_gate_scale, dtype=np.float32).reshape(1, -1),
        "usc": np.ascontiguousarray(L0_ffn_up_scale, dtype=np.float32).reshape(1, -1),
        "dsc": np.ascontiguousarray(L0_ffn_down_scale, dtype=np.float32).reshape(1, -1),
    }
    nc = build_nc()
    in_maps = [dict(com, x_in=np.ascontiguousarray(x[b])) for b in range(8)]
    res = run_bass_kernel_spmd(nc, in_maps, core_ids=list(range(8)))
    return np.stack([np.asarray(r["y"], dtype=np.float32) for r in res.results], axis=0)
